# version 01
# baseline (speedup 1.0000x reference)
import numpy as np
import ml_dtypes
from contextlib import ExitStack
import concourse.bass as bass
import concourse.mybir as mybir
from concourse.bass_utils import run_bass_kernel_spmd

F32 = mybir.dt.float32
BF16 = mybir.dt.bfloat16
U8 = mybir.dt.uint8
AF = mybir.ActivationFunctionType
ALU = mybir.AluOpType
AX = mybir.AxisListType

NCORES = 8
D = 1024
KD = 8
BATCH = 16
SEQ = 2048
PPC = SEQ // NCORES
TOK = 4096
BPC = BATCH // NCORES


class Prog:

    def __init__(self, name="prog"):
        self.nc = bass.Bass("TRN2", target_bir_lowering=False, debug=False)
        self.ctx = ExitStack()
        self.sems = {}
        self.cnt = {}
        self.block = None
        self._sbuf_bytes = 0

    def start_block(self):
        self.block = self.ctx.enter_context(self.nc.Block())

    def sem(self, name):
        if name not in self.sems:
            self.sems[name] = self.ctx.enter_context(self.nc.semaphore(name))
            self.cnt[name] = 0
        return self.sems[name]

    def inc(self, inst, name, k=1):
        inst.then_inc(self.sem(name), k)
        self.cnt[name] += k
        return self.cnt[name]

    def sbuf(self, name, shape, dt):
        per_part = int(np.prod(shape[1:])) * mybir.dt.size(dt)
        self._sbuf_bytes += per_part
        return self.ctx.enter_context(self.nc.sbuf_tensor(name, list(shape), dt))

    def psum(self, name, shape, dt=F32):
        return self.ctx.enter_context(self.nc.psum_tensor(name, list(shape), dt))

    def close(self):
        self.ctx.close()


def vec_chunks(v):
    n = v.shape[0]
    return np.ascontiguousarray(v.reshape(n // 128, 128).T).astype(np.float32)



def build_ffn_program(debug=False):
    p = Prog("ffn")
    nc = p.nc
    xin = nc.dram_tensor("xin", [D, TOK], F32, kind="ExternalInput").ap()
    W = {}
    for w in ("wf", "wi", "wc", "uf", "ui", "uc", "wo", "uo"):
        W[w] = nc.dram_tensor(w, [D, D], F32, kind="ExternalInput").ap()
    V = {}
    for v in ("bf", "bi", "bc", "bo", "in_c"):
        V[v] = nc.dram_tensor(v, [128, KD], F32, kind="ExternalInput").ap()
    xout = nc.dram_tensor("xout", [D, TOK], F32, kind="ExternalOutput").ap()
    if debug:
        dbg_c = nc.dram_tensor("dbg_c", [128, KD * PPC], F32, kind="ExternalOutput").ap()
        dbg_xn = nc.dram_tensor("dbg_xn", [D, TOK], BF16, kind="ExternalOutput").ap()
        dbg_mB = nc.dram_tensor("dbg_mB", [128, TOK], BF16, kind="ExternalOutput").ap()
        dbg_rB = nc.dram_tensor("dbg_rB", [128, TOK], BF16, kind="ExternalOutput").ap()
    Wbf = {w: nc.dram_tensor(f"{w}_bf", [D, D], BF16, kind="Internal").ap()
           for w in W}
    xn_dram = nc.dram_tensor("xn_dram", [D, TOK], BF16, kind="Internal").ap()

    WSLOT = 1024
    wt = p.sbuf("wt", [128, 48 * WSLOT], BF16)
    mB = p.sbuf("mB", [128, TOK], BF16)
    rB = p.sbuf("rB", [128, TOK], BF16)
    xsb = p.sbuf("xsb", [128, KD * 512], F32)
    xbb = p.sbuf("xbb", [128, KD * 512], BF16)
    sqs = p.sbuf("sqs", [128, 2 * 512], BF16)
    xld = p.sbuf("xld", [128, KD * PPC], F32)
    xnt = p.sbuf("xnt", [128, 2 * KD * PPC], BF16)
    cst = p.sbuf("cst", [128, KD * PPC], F32)
    cbf = p.sbuf("cbf", [128, KD * PPC], BF16)
    fga = xsb[:, 0:KD * PPC]
    iga = xsb[:, KD * PPC:2 * KD * PPC]
    gga = p.sbuf("gga", [128, KD * PPC], F32)
    xnscr = p.sbuf("xnscr", [128, PPC], F32)
    t3scr = p.sbuf("t3scr", [128, PPC], F32)
    hsc = p.sbuf("hsc", [128, 2 * 512], F32)
    x1sc = p.sbuf("x1sc", [128, 2 * 512], F32)
    ones_col = p.sbuf("ones_col", [128, 1], BF16)
    ones_row = p.sbuf("ones_row", [1, 128], BF16)
    onesf = p.sbuf("onesf", [128, PPC], F32)
    rowA = p.sbuf("rowA", [1, 512], F32)
    rowB = p.sbuf("rowB", [1, 512], F32)
    rowC = p.sbuf("rowC", [1, 512], F32)
    rowMb = p.sbuf("rowMb", [1, 512], BF16)
    rowRb = p.sbuf("rowRb", [1, 512], BF16)
    vecs = {v: p.sbuf(f"v_{v}", [128, KD], F32) for v in V}
    epsr = p.sbuf("epsr", [128, 1], F32)
    assert p._sbuf_bytes <= 206000, p._sbuf_bytes

    ps = [p.psum(f"ps{i}", [128, 512], F32) for i in range(8)]

    p.start_block()
    block = p.block

    def wtile(w_idx, k, m):
        base = (w_idx * KD + k) * WSLOT
        return wt[:, base + m * 128: base + m * 128 + 128]

    LOOPW = ("wf", "wi", "wc", "uf", "ui", "uc")

    @block.gpsimd
    def _(g):
        for w in ("wf", "wi", "wc", "uf", "ui", "uc", "wo", "uo"):
            p.inc(g.dma_start(out=Wbf[w][:], in_=W[w][:]), "ld_cast", 16)
        for v in V:
            p.inc(g.dma_start(out=vecs[v][:], in_=V[v][:]), "ld_v", 16)
        g.memset(epsr[:], 1e-5)
        g.memset(ones_col[:], 1.0)
        g.memset(ones_row[:], 1.0)
        p.inc(g.memset(onesf[:], 1.0), "ld_v", 1)

    @block.scalar
    def _(s):
        s.wait_ge(p.sem("ld_cast"), 16 * 8)
        for wi_, w in enumerate(LOOPW):
            for k in range(KD):
                p.inc(s.dma_start(out=wt[:, (wi_ * KD + k) * WSLOT:(wi_ * KD + k + 1) * WSLOT],
                                  in_=Wbf[w][:, k * 128:(k + 1) * 128], transpose=True),
                      "ld_wt", 16)

    NB = TOK // 512
    for nb in range(NB):
        @block.sync
        def _(s, nb=nb):
            if nb > 0:
                s.wait_ge(p.sem("ev_sq"), nb * KD)
                s.wait_ge(p.sem("v_cast"), nb)
            p.inc(s.dma_start(
                out=xsb[:].rearrange("q (k c) -> q k c", k=KD),
                in_=xin.rearrange("(k q) t -> q k t", q=128)[:, :, nb * 512:(nb + 1) * 512]),
                "ld_x", 16)

        @block.vector
        def _(v, nb=nb):
            v.wait_ge(p.sem("ld_x"), p.cnt["ld_x"])
            for k in range(KD):
                inst = v.tensor_copy(xbb[:, k * 512:(k + 1) * 512], xsb[:, k * 512:(k + 1) * 512])
                if k == KD - 1:
                    p.inc(inst, "v_cast", 1)

        @block.scalar
        def _(s, nb=nb):
            for k in range(KD):
                if nb > 0 or k >= 2:
                    s.wait_ge(p.sem("mm_sq"), nb * KD + k - 1)
                s.wait_ge(p.sem("ld_x"), p.cnt["ld_x"])
                p.inc(s.activation(sqs[:, (k % 2) * 512:(k % 2 + 1) * 512],
                                   xsb[:, k * 512:(k + 1) * 512], AF.Square), "ev_sq", 1)

        @block.tensor
        def _(t, nb=nb):
            t.wait_ge(p.sem("v_cast"), nb + 1)
            if nb > 0:
                t.wait_ge(p.sem("ev"), 2 * nb)
            for k in range(KD):
                t.matmul(ps[0][0:1, :], ones_col[:], xbb[:, k * 512:(k + 1) * 512],
                         start=(k == 0), stop=(k == KD - 1))
            for k in range(KD):
                t.wait_ge(p.sem("ev_sq"), nb * KD + k + 1)
                inst = t.matmul(ps[1][0:1, :], ones_col[:], sqs[:, (k % 2) * 512:(k % 2 + 1) * 512],
                                start=(k == 0), stop=(k == KD - 1))
                p.inc(inst, "mm_sq", 1)

        @block.scalar
        def _(s, nb=nb):
            s.wait_ge(p.sem("mm_sq"), (nb + 1) * KD)
            s.mul(rowA[:], ps[0][0:1, :], 1.0 / D)
            p.inc(s.mul(rowB[:], ps[1][0:1, :], 1.0 / D), "ev", 1)

        @block.vector
        def _(v, nb=nb):
            v.wait_ge(p.sem("ev"), p.cnt["ev"])
            v.tensor_mul(rowC[:], rowA[:], rowA[:])
            p.inc(v.tensor_sub(rowC[:], rowB[:], rowC[:]), "v_row", 1)

        @block.scalar
        def _(s, nb=nb):
            s.wait_ge(p.sem("v_row"), p.cnt["v_row"])
            p.inc(s.activation(rowB[:], rowC[:], AF.Sqrt, bias=epsr[0:1, 0:1]), "ev", 1)

        @block.vector
        def _(v, nb=nb):
            v.wait_ge(p.sem("ev"), p.cnt["ev"])
            v.reciprocal(rowC[:], rowB[:])
            v.tensor_copy(rowMb[:], rowA[:])
            p.inc(v.tensor_copy(rowRb[:], rowC[:]), "v_row", 1)

        @block.tensor
        def _(t, nb=nb):
            t.wait_ge(p.sem("v_row"), p.cnt["v_row"])
            if nb > 0:
                t.wait_ge(p.sem("v_mb"), nb)
            t.matmul(ps[2][:], ones_row[:], rowMb[:], start=True, stop=True)
            p.inc(t.matmul(ps[3][:], ones_row[:], rowRb[:], start=True, stop=True), "mm", 1)

        @block.vector
        def _(v, nb=nb):
            v.wait_ge(p.sem("mm"), p.cnt["mm"])
            v.tensor_copy(mB[:, nb * 512:(nb + 1) * 512], ps[2][:])
            p.inc(v.tensor_copy(rB[:, nb * 512:(nb + 1) * 512], ps[3][:]), "v_mb", 1)

    stats_mm = p.cnt["mm"]

    @block.vector
    def _(v):
        v.wait_ge(p.sem("ld_v"), p.cnt["ld_v"])
        for m in range(KD):
            v.tensor_scalar(cst[:, m * PPC:(m + 1) * PPC], onesf[:],
                            vecs["in_c"][:, m:m + 1], None, ALU.mult)
            p.inc(v.tensor_copy(cbf[:, m * PPC:(m + 1) * PPC], cst[:, m * PPC:(m + 1) * PPC]),
                  "v_c", 1)

    @block.sync
    def _(s):
        p.inc(s.dma_start(
            out=xld[:].rearrange("q (k c) -> q k c", k=KD),
            in_=xin.rearrange("(k q) t -> q k t", q=128)[:, :, 0:PPC]), "ld_x", 16)

    @block.vector
    def _(v):
        v.wait_ge(p.sem("ld_x"), p.cnt["ld_x"])
        v.wait_ge(p.sem("v_mb"), NB)
        for k in range(KD):
            v.tensor_sub(xnscr[:], xld[:, k * PPC:(k + 1) * PPC], mB[:, 0:PPC])
            p.inc(v.tensor_mul(xnt[:, k * PPC:(k + 1) * PPC], xnscr[:], rB[:, 0:PPC]), "v_xn", 1)

    T = BATCH
    NGRP = 3 * KD
    mm_base = p.cnt["mm"]
    GATES = (("wf", "bf", AF.Sigmoid, fga), ("wi", "bi", AF.Sigmoid, iga), ("wc", "bc", AF.Tanh, gga))

    for t in range(T):
        par = t % 2

        @block.sync
        def _(s, t=t):
            if t + 1 < T:
                s.wait_ge(p.sem("v_xn"), 8 * (t + 1))
                p.inc(s.dma_start(
                    out=xld[:].rearrange("q (k c) -> q k c", k=KD),
                    in_=xin.rearrange("(k q) t -> q k t", q=128)[:, :, (t + 1) * PPC:(t + 2) * PPC]),
                    "ld_x", 16)

        @block.tensor
        def _(t_, t=t):
            t_.wait_ge(p.sem("ld_wt"), 16 * 48)
            t_.wait_ge(p.sem("v_xn"), 8 * (t + 1))
            t_.wait_ge(p.sem("v_c"), 8 * (t + 1))
            for gi, (wname, bname, af, gbuf) in enumerate(GATES):
                uname = "u" + wname[1]
                for m in range(KD):
                    gidx = NGRP * t + gi * KD + m
                    if gidx >= 6:
                        t_.wait_ge(p.sem("ev_g"), gidx - 5)
                    bank = ps[gidx % 6]
                    for k in range(KD):
                        t_.matmul(bank[:, 0:PPC], wtile(LOOPW.index(wname), k, m),
                                  xnt[:, par * KD * PPC + k * PPC: par * KD * PPC + (k + 1) * PPC],
                                  start=(k == 0), stop=False)
                    for k in range(KD):
                        inst = t_.matmul(bank[:, 0:PPC], wtile(LOOPW.index(uname), k, m),
                                         cbf[:, k * PPC:(k + 1) * PPC],
                                         start=False, stop=(k == KD - 1))
                    p.inc(inst, "mm", 1)

        @block.scalar
        def _(s, t=t):
            for gi, (wname, bname, af, gbuf) in enumerate(GATES):
                for m in range(KD):
                    gidx = NGRP * t + gi * KD + m
                    s.wait_ge(p.sem("mm"), mm_base + gidx + 1)
                    if t > 0:
                        s.wait_ge(p.sem("v_c"), 8 + 8 * (t - 1) + m + 1)
                    p.inc(s.activation(gbuf[:, m * PPC:(m + 1) * PPC],
                                       ps[gidx % 6][:, 0:PPC], af,
                                       bias=vecs[bname][:, m:m + 1]), "ev_g", 1)

        @block.vector
        def _(v, t=t):
            ev_base = NGRP * t
            v.wait_ge(p.sem("mm"), mm_base + NGRP * (t + 1))
            for m in range(KD):
                v.wait_ge(p.sem("ev_g"), ev_base + 17 + m)
                sl = slice(m * PPC, (m + 1) * PPC)
                v.tensor_mul(iga[:, sl], iga[:, sl], gga[:, sl])
                v.tensor_mul(t3scr[:], fga[:, sl], cst[:, sl])
                v.tensor_add(cst[:, sl], t3scr[:], iga[:, sl])
                p.inc(v.tensor_copy(cbf[:, sl], cst[:, sl]), "v_c", 1)
            if t + 1 < T:
                v.wait_ge(p.sem("ld_x"), p.cnt["ld_x"])
                v.wait_ge(p.sem("mm"), mm_base + NGRP * t)
                v.wait_ge(p.sem("st_xn"), 16 * t)
                np_ = (t + 1) % 2
                c0 = (t + 1) * PPC
                for k in range(KD):
                    v.tensor_sub(xnscr[:], xld[:, k * PPC:(k + 1) * PPC], mB[:, c0:c0 + PPC])
                    p.inc(v.tensor_mul(xnt[:, np_ * KD * PPC + k * PPC: np_ * KD * PPC + (k + 1) * PPC],
                                       xnscr[:], rB[:, c0:c0 + PPC]), "v_xn", 1)

        @block.gpsimd
        def _(g, t=t):
            g.wait_ge(p.sem("v_xn"), 8 * (t + 1))
            p.inc(g.dma_start(out=xn_dram.rearrange("(k q) t -> q k t", q=128)[:, :, t * PPC:(t + 1) * PPC],
                              in_=xnt[:, par * KD * PPC:(par + 1) * KD * PPC]
                              .rearrange("q (k c) -> q k c", k=KD)), "st_xn", 16)

    loop_mm = p.cnt["mm"]
    loop_ev = p.cnt["ev_g"]

    @block.scalar
    def _(s):
        s.wait_ge(p.sem("mm"), loop_mm)
        for wi_, w in enumerate(("wo", "uo")):
            for k in range(KD):
                p.inc(s.dma_start(out=wt[:, (wi_ * KD + k) * WSLOT:(wi_ * KD + k + 1) * WSLOT],
                                  in_=Wbf[w][:, k * 128:(k + 1) * 128], transpose=True),
                      "ld_wt", 16)

    for nb in range(NB):
        @block.sync
        def _(s, nb=nb):
            if nb == 0:
                s.wait_ge(p.sem("st_xn"), 16 * T)
                s.wait_ge(p.sem("v_c"), 8 * (T + 1))
            else:
                s.wait_ge(p.sem("v_o"), nb * KD)
            p.inc(s.dma_start(
                out=xsb[:].rearrange("q (k c) -> q k c", k=KD),
                in_=xin.rearrange("(k q) t -> q k t", q=128)[:, :, nb * 512:(nb + 1) * 512]),
                "ld_x", 16)
            p.inc(s.dma_start(
                out=xbb[:].rearrange("q (k c) -> q k c", k=KD),
                in_=xn_dram.rearrange("(k q) t -> q k t", q=128)[:, :, nb * 512:(nb + 1) * 512]),
                "ld_xn", 16)

        @block.tensor
        def _(t_, nb=nb):
            t_.wait_ge(p.sem("ld_wt"), 16 * 64)
            t_.wait_ge(p.sem("ld_xn"), p.cnt["ld_xn"])
            t_.wait_ge(p.sem("v_c"), 8 * (T + 1))
            for m in range(KD):
                gidx = nb * KD + m
                if gidx >= 2:
                    t_.wait_ge(p.sem("ev_e"), gidx - 1)
                bank = ps[6 + gidx % 2]
                for k in range(KD):
                    t_.matmul(bank[:], wtile(0, k, m), xbb[:, k * 512:(k + 1) * 512],
                              start=(k == 0), stop=False)
                for k in range(KD):
                    rhs = cbf[:, k * PPC:(k + 1) * PPC].rearrange("q (r c) -> q r c", r=1).broadcast_to([128, 2, PPC])
                    inst = t_.matmul(bank[:].rearrange("q (r c) -> q r c", r=2),
                                     wtile(1, k, m), rhs, start=False, stop=(k == KD - 1))
                p.inc(inst, "mm", 1)

        @block.scalar
        def _(s, nb=nb):
            for m in range(KD):
                gidx = nb * KD + m
                s.wait_ge(p.sem("mm"), loop_mm + gidx + 1)
                if gidx >= 2:
                    s.wait_ge(p.sem("v_o"), gidx - 1)
                p.inc(s.activation(hsc[:, (gidx % 2) * 512:(gidx % 2 + 1) * 512],
                                   ps[6 + gidx % 2][:], AF.Tanh,
                                   bias=vecs["bo"][:, m:m + 1]), "ev_e", 1)

        @block.vector
        def _(v, nb=nb):
            for m in range(KD):
                gidx = nb * KD + m
                v.wait_ge(p.sem("ev_e"), gidx + 1)
                v.wait_ge(p.sem("ld_x"), p.cnt["ld_x"])
                if gidx >= 2:
                    v.wait_ge(p.sem("st_out"), 16 * (gidx - 1))
                p.inc(v.scalar_tensor_tensor(x1sc[:, (gidx % 2) * 512:(gidx % 2 + 1) * 512],
                                             hsc[:, (gidx % 2) * 512:(gidx % 2 + 1) * 512],
                                             0.5, xsb[:, m * 512:(m + 1) * 512],
                                             ALU.mult, ALU.add), "v_o", 1)

        @block.gpsimd
        def _(g, nb=nb):
            for m in range(KD):
                gidx = nb * KD + m
                g.wait_ge(p.sem("v_o"), gidx + 1)
                p.inc(g.dma_start(out=xout[m * 128:(m + 1) * 128, nb * 512:(nb + 1) * 512],
                                  in_=x1sc[:, (gidx % 2) * 512:(gidx % 2 + 1) * 512]),
                      "st_out", 16)

    if debug:
        @block.gpsimd
        def _(g):
            g.wait_ge(p.sem("v_c"), 8 * (T + 1))
            p.inc(g.dma_start(out=dbg_c[:], in_=cst[:]), "st_out", 16)
            g.wait_ge(p.sem("st_xn"), 16 * T)
            p.inc(g.dma_start(out=dbg_xn[:], in_=xn_dram[:]), "st_out", 16)
            g.wait_ge(p.sem("v_mb"), NB)
            p.inc(g.dma_start(out=dbg_mB[:], in_=mB[:]), "st_out", 16)
            p.inc(g.dma_start(out=dbg_rB[:], in_=rB[:]), "st_out", 16)

    @block.sync
    def _(s):
        s.wait_ge(p.sem("st_out"), p.cnt["st_out"])

    p.close()
    return p.nc




def emit_ln_stats(p, block, ps, xin, mB, rB, ntok, bufs, pre_wait=None):
    xsb, xbb, sqs = bufs["xsb"], bufs["xbb"], bufs["sqs"]
    ones_col, ones_row, epsr = bufs["ones_col"], bufs["ones_row"], bufs["epsr"]
    rowA, rowB, rowC = bufs["rowA"], bufs["rowB"], bufs["rowC"]
    rowMb, rowRb = bufs["rowMb"], bufs["rowRb"]
    NBs = ntok // 512
    for nb in range(NBs):
        @block.sync
        def _(s, nb=nb):
            if nb == 0 and pre_wait is not None:
                pre_wait(s)
            if nb > 0:
                s.wait_ge(p.sem("ln_sq"), nb * KD)
                s.wait_ge(p.sem("ln_cast"), nb)
            p.inc(s.dma_start(
                out=xsb.rearrange("q (k c) -> q k c", k=KD),
                in_=xin.rearrange("(k q) t -> q k t", q=128)[:, :, nb * 512:(nb + 1) * 512]),
                "ln_ldx", 16)

        @block.vector
        def _(v, nb=nb):
            v.wait_ge(p.sem("ln_ldx"), p.cnt["ln_ldx"])
            for k in range(KD):
                inst = v.tensor_copy(xbb[:, k * 512:(k + 1) * 512], xsb[:, k * 512:(k + 1) * 512])
                if k == KD - 1:
                    p.inc(inst, "ln_cast", 1)

        @block.scalar
        def _(s, nb=nb):
            for k in range(KD):
                if nb > 0 or k >= 2:
                    s.wait_ge(p.sem("ln_mmsq"), nb * KD + k - 1)
                s.wait_ge(p.sem("ln_ldx"), p.cnt["ln_ldx"])
                p.inc(s.activation(sqs[:, (k % 2) * 512:(k % 2 + 1) * 512],
                                   xsb[:, k * 512:(k + 1) * 512], AF.Square), "ln_sq", 1)

        @block.tensor
        def _(t, nb=nb):
            t.wait_ge(p.sem("ln_cast"), nb + 1)
            if nb > 0:
                t.wait_ge(p.sem("ln_ev"), 2 * nb)
            for k in range(KD):
                t.matmul(ps[0][0:1, :], ones_col[:], xbb[:, k * 512:(k + 1) * 512],
                         start=(k == 0), stop=(k == KD - 1))
            for k in range(KD):
                t.wait_ge(p.sem("ln_sq"), nb * KD + k + 1)
                inst = t.matmul(ps[1][0:1, :], ones_col[:], sqs[:, (k % 2) * 512:(k % 2 + 1) * 512],
                                start=(k == 0), stop=(k == KD - 1))
                p.inc(inst, "ln_mmsq", 1)

        @block.scalar
        def _(s, nb=nb):
            s.wait_ge(p.sem("ln_mmsq"), (nb + 1) * KD)
            s.mul(rowA[:], ps[0][0:1, :], 1.0 / D)
            p.inc(s.mul(rowB[:], ps[1][0:1, :], 1.0 / D), "ln_ev", 1)

        @block.vector
        def _(v, nb=nb):
            v.wait_ge(p.sem("ln_ev"), p.cnt["ln_ev"])
            v.tensor_mul(rowC[:], rowA[:], rowA[:])
            p.inc(v.tensor_sub(rowC[:], rowB[:], rowC[:]), "ln_row", 1)

        @block.scalar
        def _(s, nb=nb):
            s.wait_ge(p.sem("ln_row"), p.cnt["ln_row"])
            p.inc(s.activation(rowB[:], rowC[:], AF.Sqrt, bias=epsr[0:1, 0:1]), "ln_ev", 1)

        @block.vector
        def _(v, nb=nb):
            v.wait_ge(p.sem("ln_ev"), p.cnt["ln_ev"])
            v.reciprocal(rowC[:], rowB[:])
            v.tensor_copy(rowMb[:], rowA[:])
            p.inc(v.tensor_copy(rowRb[:], rowC[:]), "ln_row", 1)

        @block.tensor
        def _(t, nb=nb):
            t.wait_ge(p.sem("ln_row"), p.cnt["ln_row"])
            if nb > 0:
                t.wait_ge(p.sem("v_mb"), nb)
            t.matmul(ps[2][:], ones_row[:], rowMb[:], start=True, stop=True)
            p.inc(t.matmul(ps[3][:], ones_row[:], rowRb[:], start=True, stop=True), "ln_mm", 1)

        @block.vector
        def _(v, nb=nb):
            v.wait_ge(p.sem("ln_mm"), p.cnt["ln_mm"])
            v.tensor_copy(mB[:, nb * 512:(nb + 1) * 512], ps[2][:])
            p.inc(v.tensor_copy(rB[:, nb * 512:(nb + 1) * 512], ps[3][:]), "v_mb", 1)



def build_mamba_program(debug=False):
    p = Prog("mamba")
    nc = p.nc
    xin = nc.dram_tensor("xin", [D, TOK], F32, kind="ExternalInput").ap()
    Win = nc.dram_tensor("W_in", [D, 2048], F32, kind="ExternalInput").ap()
    Wxp = nc.dram_tensor("W_xproj", [D, 96], F32, kind="ExternalInput").ap()
    Wdt = nc.dram_tensor("W_dt", [64, D], F32, kind="ExternalInput").ap()
    Wout = nc.dram_tensor("W_out", [2048, D], F32, kind="ExternalInput").ap()
    ALOG = nc.dram_tensor("A_log", [128, KD * 16], F32, kind="ExternalInput").ap()
    KX = nc.dram_tensor("K_x", [128, KD * 4], F32, kind="ExternalInput").ap()
    KZ = nc.dram_tensor("K_z", [128, KD * 4], F32, kind="ExternalInput").ap()
    V = {}
    for v in ("b_dt", "D_vec", "b_x", "b_z"):
        V[v] = nc.dram_tensor(v, [128, KD], F32, kind="ExternalInput").ap()
    xout = nc.dram_tensor("xout", [D, TOK], F32, kind="ExternalOutput").ap()
    kind_dbg = "ExternalOutput" if debug else "Internal"
    z_dram = nc.dram_tensor("z_dram", [BPC, D, SEQ], BF16, kind=kind_dbg).ap()
    u_dram = nc.dram_tensor("u_dram", [BPC, D, SEQ], BF16, kind=kind_dbg).ap()
    y_dram = nc.dram_tensor("y_dram", [BPC, D, SEQ], BF16, kind=kind_dbg).ap()
    bc_dram = nc.dram_tensor("bc_dram", [BPC, 32, SEQ], BF16, kind=kind_dbg).ap()

    LW = 2048
    wsb = p.sbuf("wsb", [128, 16 * 1024], BF16)
    wxpb = p.sbuf("wxpb", [128, KD * 96], BF16)
    wdtb = p.sbuf("wdtb", [64, 1024], BF16)
    mB = p.sbuf("mB", [128, TOK], BF16)
    rB = p.sbuf("rB", [128, TOK], BF16)
    xsb = p.sbuf("xsb", [128, KD * 512], F32)
    xbb = p.sbuf("xbb", [128, KD * 512], BF16)
    xnm = p.sbuf("xnm", [128, KD * SEQ], BF16)
    uq = p.sbuf("uq", [128, KD * 512], BF16)
    ptil = p.sbuf("ptil", [128, SEQ + 4], F32)
    zscr = p.sbuf("zscr", [128, SEQ], BF16)
    bcb = p.sbuf("bcb", [128, 2 * 2 * SEQ], BF16)
    dab = p.sbuf("dab", [128, 2 * SEQ], F32)
    cvt = dab[:, 0:SEQ]
    dbu = p.sbuf("dbu", [128, SEQ], F32)
    hbf = p.sbuf("hbf", [128, 2 * SEQ], BF16)
    ytb = p.sbuf("ytb", [128, SEQ], F32)
    scr1 = p.sbuf("scr1", [128, 1024], F32)
    scr2 = p.sbuf("scr2", [128, 1024], F32)
    dtb = p.sbuf("dtb", [64, SEQ], BF16)

    asb = p.sbuf("asb", [128, KD * 16], F32)
    kxb = p.sbuf("kxb", [128, KD * 4], F32)
    kzb = p.sbuf("kzb", [128, KD * 4], F32)
    ob = p.sbuf("ob", [128, 2 * 512], F32)
    bcstg = ob[:]
    ones_col = p.sbuf("ones_col", [128, 1], BF16)
    ones_row = p.sbuf("ones_row", [1, 128], BF16)
    negone = p.sbuf("negone", [128, 1], F32)
    onec = p.sbuf("onec", [128, 1], F32)
    epsr = p.sbuf("epsr", [128, 1], F32)
    vecs = {v: p.sbuf(f"v_{v}", [128, KD], F32) for v in V}
    assert p._sbuf_bytes <= 226000, p._sbuf_bytes

    dlt = xsb[:, 0:SEQ]
    gbuf = xsb[:, SEQ:2 * SEQ]
    yac = xbb[:].bitcast(F32)[:, 0:SEQ]
    sqs = scr1[:].bitcast(BF16)[:, 0:2 * 512]
    rowA = scr2[:, 0:512][0:1, :]
    rowB = scr2[:, 512:1024][0:1, :]
    rowC = scr1[:, 512:1024][0:1, :]
    rowMb = dab[:].bitcast(BF16)[0:1, 0:512]
    rowRb = dab[:].bitcast(BF16)[0:1, 512:1024]
    xres0 = xsb[:, 0:512]
    xres1 = xsb[:, 512:1024]

    ps = [p.psum(f"ps{i}", [128, 512], F32) for i in range(8)]

    p.start_block()
    block = p.block
    for nm in ("v_scan", "pl_ym", "v_ya", "sc_da", "v_d", "ev_sp", "ev_ps", "v_g",
               "st_y", "st_bc", "v_db", "ld_u", "ld_bc", "cv_done", "ev_gel", "st_uz",
               "ev_pt", "ev_xz", "mm", "ld_yz", "ld_res", "v_o", "st_out", "ld_x",
               "v_xn", "ld_w", "ld_v", "v_mb"):
        p.sem(nm)
    p.cnt["p2g"] = 0
    p.cnt["p6g"] = 0

    @block.gpsimd
    def _(g):
        for k in range(KD):
            p.inc(g.dma_start(out=wsb[:, k * LW:(k + 1) * LW], in_=Win[k * 128:(k + 1) * 128, :]),
                  "ld_w", 16)
        for k in range(KD):
            p.inc(g.dma_start(out=wxpb[:, k * 96:(k + 1) * 96], in_=Wxp[k * 128:(k + 1) * 128, :]),
                  "ld_w", 16)
        p.inc(g.dma_start(out=wdtb[:], in_=Wdt[:]), "ld_w", 16)
        for v in V:
            p.inc(g.dma_start(out=vecs[v][:], in_=V[v][:]), "ld_v", 16)
        p.inc(g.dma_start(out=kxb[:], in_=KX[:]), "ld_v", 16)
        p.inc(g.dma_start(out=kzb[:], in_=KZ[:]), "ld_v", 16)
        p.inc(g.dma_start(out=asb[:], in_=ALOG[:]), "ld_v", 16)
        g.memset(ones_col[:], 1.0)
        g.memset(ones_row[:], 1.0)
        g.memset(negone[:], -1.0)
        g.memset(onec[:], 1.0)
        g.memset(epsr[:], 1e-5)
        g.memset(ptil[:, 0:1], 0.0)
        p.inc(g.memset(ptil[:, SEQ + 1:SEQ + 4], 0.0), "ld_v", 1)

    @block.scalar
    def _(s):
        s.wait_ge(p.sem("ld_v"), p.cnt["ld_v"])
        p.inc(s.activation(asb[:], asb[:], AF.Exp), "ev_a", 1)

    @block.vector
    def _(v):
        v.wait_ge(p.sem("ev_a"), 1)
        p.inc(v.tensor_scalar(asb[:], asb[:], -1.0, None, ALU.mult), "v_a", 1)

    emit_ln_stats(p, block, ps, xin, mB, rB, TOK,
                  dict(xsb=xsb[:], xbb=xbb[:], sqs=sqs, ones_col=ones_col,
                       ones_row=ones_row, epsr=epsr, rowA=rowA, rowB=rowB,
                       rowC=rowC, rowMb=rowMb, rowRb=rowRb))

    NB4 = SEQ // 512
    mm_after_p2 = {}
    mm_after_p3 = {}

    for b in range(BPC):
        c0 = b * SEQ

        for k in range(KD):
            half = k % 2

            @block.sync
            def _(s, b=b, k=k, half=half, c0=c0):
                if b == 0 and k == 0:
                    s.wait_ge(p.sem("ln_sq"), KD * (TOK // 512))
                    s.wait_ge(p.sem("ln_cast"), TOK // 512)
                if b == 1 and k == 0:
                    s.wait_ge(p.sem("v_scan"), p.cnt["v_scan"])
                    s.wait_ge(p.sem("sc_da"), p.cnt["sc_da"])
                if b * KD + k >= 2:
                    s.wait_ge(p.sem("v_xn"), b * KD + k - 1)
                p.inc(s.dma_start(out=xsb[:, half * SEQ:(half + 1) * SEQ],
                                  in_=xin[k * 128:(k + 1) * 128, c0:c0 + SEQ]), "ld_x", 16)

            @block.vector
            def _(v, b=b, k=k, half=half, c0=c0):
                v.wait_ge(p.sem("ld_x"), p.cnt["ld_x"])
                if b == 0 and k == 0:
                    v.wait_ge(p.sem("v_mb"), TOK // 512)
                if b == 1 and k == 0:
                    v.wait_ge(p.sem("mm"), mm_after_p2[0])
                v.tensor_sub(xsb[:, half * SEQ:(half + 1) * SEQ],
                             xsb[:, half * SEQ:(half + 1) * SEQ], mB[:, c0:c0 + SEQ])
                p.inc(v.tensor_mul(xnm[:, k * SEQ:(k + 1) * SEQ],
                                   xsb[:, half * SEQ:(half + 1) * SEQ], rB[:, c0:c0 + SEQ]),
                      "v_xn", 1)

        for ic in range(16):
            xhalf = ic < 8

            @block.tensor
            def _(t, b=b, ic=ic):
                if b == 0 and ic == 0:
                    t.wait_ge(p.sem("ld_w"), 16 * 8)
                    t.wait_ge(p.sem("v_mb"), TOK // 512)
                t.wait_ge(p.sem("v_xn"), KD * (b + 1))
                for nb4 in range(NB4):
                    gi = p.cnt["p2g"]
                    if gi >= 4:
                        t.wait_ge(p.sem("ev_xz"), gi - 3)
                    bank = ps[gi % 4]
                    for k in range(KD):
                        inst = t.matmul(bank[:], wsb[:, k * LW + ic * 128:k * LW + ic * 128 + 128],
                                        xnm[:, k * SEQ + nb4 * 512:k * SEQ + (nb4 + 1) * 512],
                                        start=(k == 0), stop=(k == KD - 1))
                    p.inc(inst, "mm", 1)
                    p.cnt["p2g"] = p.cnt.get("p2g", 0) + 1

            @block.vector
            def _(v, b=b, ic=ic):
                gbase = p.cnt["p2g"] - NB4
                mmbase = p.cnt["mm"] - NB4
                if ic > 0 or b > 0:
                    v.wait_ge(p.sem("cv_done"), p.cnt["cv_done"])
                for nb4 in range(NB4):
                    v.wait_ge(p.sem("mm"), mmbase + nb4 + 1)
                    p.inc(v.tensor_copy(ptil[:, 1 + nb4 * 512:1 + (nb4 + 1) * 512],
                                        ps[(gbase + nb4) % 4][:]), "ev_xz", 1)
                p.inc(v.engine_nop(), "ev_pt", 1)

            kv = kxb if xhalf else kzb
            icc = ic if xhalf else ic - 8
            @block.vector
            def _(v, b=b, ic=ic, kv=kv, icc=icc):
                if ic > 0 or b > 0:
                    v.wait_ge(p.sem("ev_gel"), p.cnt["ev_gel"])
                v.tensor_scalar(cvt, ptil[:, 0:SEQ], kv[:, icc * 4:icc * 4 + 1],
                                None, ALU.mult)
                for tap in range(1, 4):
                    inst = v.scalar_tensor_tensor(cvt, ptil[:, tap:tap + SEQ],
                                                  kv[:, icc * 4 + tap:icc * 4 + tap + 1],
                                                  cvt, ALU.mult, ALU.add)
                p.inc(inst, "cv_done", 1)

            @block.scalar
            def _(s, b=b, ic=ic, xhalf=xhalf, icc=icc):
                s.wait_ge(p.sem("cv_done"), p.cnt["cv_done"])
                gl = b * 16 + ic
                if gl >= 1:
                    s.wait_ge(p.sem("st_uz"), 16 * gl)
                bias = vecs["b_x"][:, icc:icc + 1] if xhalf else vecs["b_z"][:, icc:icc + 1]
                p.inc(s.activation(zscr[:, 0:SEQ], cvt,
                                   AF.Gelu, bias=bias), "ev_gel", 1)

            @block.gpsimd
            def _(g, b=b, ic=ic, xhalf=xhalf, icc=icc):
                gl = b * 16 + ic
                g.wait_ge(p.sem("ev_gel"), p.cnt["ev_gel"])
                tgt = u_dram if xhalf else z_dram
                p.inc(g.dma_start(out=tgt[b, icc * 128:(icc + 1) * 128, :],
                                  in_=zscr[:, 0:SEQ]), "st_uz", 16)

        mm_after_p2[b] = p.cnt["mm"]

        for nb4 in range(NB4):
            @block.sync
            def _(s, b=b, nb4=nb4):
                s.wait_ge(p.sem("st_uz"), 16 * (b * 16 + 8))
                if b > 0 or nb4 > 0:
                    s.wait_ge(p.sem("mm"), p.cnt["mm"])
                for k in range(KD):
                    p.inc(s.dma_start(out=uq[:, k * 512:(k + 1) * 512],
                                      in_=u_dram[b, k * 128:(k + 1) * 128, nb4 * 512:(nb4 + 1) * 512]),
                          "ld_u", 16)

            @block.tensor
            def _(t, b=b, nb4=nb4):
                t.wait_ge(p.sem("ld_u"), p.cnt["ld_u"])
                if b > 0 or nb4 > 0:
                    t.wait_ge(p.sem("v_db"), p.cnt["v_db"])
                    t.wait_ge(p.sem("sc_db"), p.cnt["sc_db"])
                for k in range(KD):
                    t.matmul(ps[4][0:64, :], wxpb[:, k * 96:k * 96 + 64],
                             uq[:, k * 512:(k + 1) * 512], start=(k == 0), stop=(k == KD - 1))
                for k in range(KD):
                    t.matmul(ps[5][0:16, :], wxpb[:, k * 96 + 64:k * 96 + 80],
                             uq[:, k * 512:(k + 1) * 512], start=(k == 0), stop=(k == KD - 1))
                for k in range(KD):
                    inst = t.matmul(ps[6][0:16, :], wxpb[:, k * 96 + 80:k * 96 + 96],
                                    uq[:, k * 512:(k + 1) * 512], start=(k == 0), stop=(k == KD - 1))
                p.inc(inst, "mm", 1)

            @block.vector
            def _(v, b=b, nb4=nb4):
                v.wait_ge(p.sem("mm"), p.cnt["mm"])
                p.inc(v.tensor_copy(dtb[0:64, nb4 * 512:(nb4 + 1) * 512], ps[4][0:64, :]),
                      "v_db", 1)

            @block.scalar
            def _(s, b=b, nb4=nb4):
                s.wait_ge(p.sem("mm"), p.cnt["mm"])
                if b > 0 or nb4 > 0:
                    s.wait_ge(p.sem("st_bc"), p.cnt["st_bc"])
                s.copy(bcstg[0:16, 0:512], ps[5][0:16, :])
                p.inc(s.copy(bcstg[0:16, 512:1024], ps[6][0:16, :]), "sc_db", 1)

            @block.gpsimd
            def _(g, b=b, nb4=nb4):
                g.wait_ge(p.sem("v_db"), p.cnt["v_db"])
                g.wait_ge(p.sem("sc_db"), p.cnt["sc_db"])
                p.inc(g.dma_start(out=bc_dram[b, 0:16, nb4 * 512:(nb4 + 1) * 512],
                                  in_=bcstg[0:16, 0:512]), "st_bc", 16)
                p.inc(g.dma_start(out=bc_dram[b, 16:32, nb4 * 512:(nb4 + 1) * 512],
                                  in_=bcstg[0:16, 512:1024]), "st_bc", 16)

        mm_after_p3[b] = p.cnt["mm"]

        for mc in range(KD):
            @block.sync
            def _(s, b=b, mc=mc):
                s.wait_ge(p.sem("st_uz"), 16 * (b * 16 + mc + 1))
                s.wait_ge(p.sem("mm"), mm_after_p3[b])
                if b > 0 or mc > 0:
                    s.wait_ge(p.sem("v_g"), p.cnt["v_g"])
                p.inc(s.dma_start(out=uq[:, 0:SEQ * 1],
                                  in_=u_dram[b, mc * 128:(mc + 1) * 128, :]), "ld_u", 16)

            for nb4 in range(NB4):
                @block.tensor
                def _(t, b=b, mc=mc, nb4=nb4):
                    if b + mc + nb4 > 0:
                        t.wait_ge(p.sem("ev_ps"), p.cnt["ev_ps"])
                    if nb4 == 0:
                        t.wait_ge(p.sem("v_scan"), p.cnt["v_scan"])
                        t.wait_ge(p.sem("sc_da"), p.cnt["sc_da"])
                    p.inc(t.matmul(ps[7][:], wdtb[:, mc * 128:(mc + 1) * 128],
                                   dtb[0:64, nb4 * 512:(nb4 + 1) * 512],
                                   start=True, stop=True), "mm", 1)

                @block.scalar
                def _(s, b=b, mc=mc, nb4=nb4):
                    s.wait_ge(p.sem("mm"), p.cnt["mm"])
                    if b + mc + nb4 > 0:
                        s.wait_ge(p.sem("v_d"), p.cnt["v_d"])
                    h2 = (nb4 % 2)
                    s.activation(scr1[:, h2 * 512:(h2 + 1) * 512], ps[7][:], AF.Abs,
                                 bias=vecs["b_dt"][:, mc:mc + 1])
                    p.inc(s.activation(scr2[:, h2 * 512:(h2 + 1) * 512], ps[7][:], AF.Relu,
                                       bias=vecs["b_dt"][:, mc:mc + 1]), "ev_ps", 1)
                    s.activation(scr1[:, h2 * 512:(h2 + 1) * 512],
                                 scr1[:, h2 * 512:(h2 + 1) * 512], AF.Exp,
                                 scale=negone[:, 0:1])
                    p.inc(s.activation(scr1[:, h2 * 512:(h2 + 1) * 512],
                                       scr1[:, h2 * 512:(h2 + 1) * 512], AF.Ln,
                                       bias=onec[:, 0:1]), "ev_sp", 1)

                @block.vector
                def _(v, b=b, mc=mc, nb4=nb4):
                    v.wait_ge(p.sem("ev_sp"), p.cnt["ev_sp"])
                    h2 = (nb4 % 2)
                    p.inc(v.tensor_add(dlt[:, nb4 * 512:(nb4 + 1) * 512],
                                       scr2[:, h2 * 512:(h2 + 1) * 512],
                                       scr1[:, h2 * 512:(h2 + 1) * 512]), "v_d", 1)

            @block.vector
            def _(v, b=b, mc=mc):
                v.wait_ge(p.sem("ld_u"), p.cnt["ld_u"])
                if b == 0 and mc == 0:
                    pass
                else:
                    v.wait_ge(p.sem("st_y"), p.cnt["st_y"])
                v.tensor_mul(gbuf, dlt, uq[:].bitcast(BF16)[:, 0:SEQ])
                p.inc(v.tensor_scalar(yac, uq[:].bitcast(BF16)[:, 0:SEQ],
                                      vecs["D_vec"][:, mc:mc + 1], None, ALU.mult), "v_g", 1)

            scan0 = p.cnt["v_scan"]
            ym0 = p.cnt["pl_ym"]
            ya0 = p.cnt["v_ya"]
            da0 = p.cnt["sc_da"]
            for n in range(16):
                @block.sync
                def _(s, b=b, mc=mc, n=n, scan0=scan0, ym0=ym0):
                    if n >= 2:
                        s.wait_ge(p.sem("v_scan"), scan0 + n - 1)
                        s.wait_ge(p.sem("pl_ym"), ym0 + n - 1)
                    elif b + mc > 0:
                        s.wait_ge(p.sem("v_scan"), scan0)
                        s.wait_ge(p.sem("pl_ym"), ym0)
                    if mc == 0 and n == 0:
                        s.wait_ge(p.sem("st_bc"), 16 * 8 * (b + 1))
                    ap = bc_dram[b].rearrange("(a r) c -> a r c", a=2)[:, n:n + 1, :]
                    ap = ap.rearrange("a r c -> (a r) c")
                    ap = ap.rearrange("(q a) c -> q a c", q=1).broadcast_to([128, 2, SEQ])
                    p.inc(s.dma_start(out=bcb[:, (n % 2) * 2 * SEQ:(n % 2 + 1) * 2 * SEQ]
                                      .rearrange("q (a c) -> q a c", a=2), in_=ap), "ld_bc", 16)

                @block.scalar
                def _(s, b=b, mc=mc, n=n, scan0=scan0, da0=da0):
                    if n == 0:
                        s.wait_ge(p.sem("v_d"), p.cnt["v_d"])
                        s.wait_ge(p.sem("v_a"), 1)
                    if n >= 2:
                        s.wait_ge(p.sem("v_scan"), scan0 + n - 1)
                    p.inc(s.activation(dab[:, (n % 2) * SEQ:(n % 2 + 1) * SEQ], dlt,
                                       AF.Exp, scale=asb[:, mc * 16 + n:mc * 16 + n + 1]),
                          "sc_da", 1)

                @block.vector
                def _(v, b=b, mc=mc, n=n, scan0=scan0, ym0=ym0, ya0=ya0, da0=da0):
                    v.wait_ge(p.sem("ld_bc"), p.cnt["ld_bc"])
                    if n == 0:
                        v.wait_ge(p.sem("v_g"), p.cnt["v_g"])
                    v.tensor_mul(dbu[:], gbuf, bcb[:, (n % 2) * 2 * SEQ:(n % 2) * 2 * SEQ + SEQ])
                    v.wait_ge(p.sem("sc_da"), da0 + n + 1)
                    if n >= 2:
                        v.wait_ge(p.sem("pl_ym"), ym0 + n - 1)
                    p.inc(v.tensor_tensor_scan(hbf[:, (n % 2) * SEQ:(n % 2 + 1) * SEQ],
                                               dab[:, (n % 2) * SEQ:(n % 2 + 1) * SEQ],
                                               dbu[:], 0.0, ALU.mult, ALU.add), "v_scan", 1)
                    if n > 0:
                        v.wait_ge(p.sem("pl_ym"), ym0 + n)
                        p.inc(v.tensor_add(yac, yac, ytb[:]), "v_ya", 1)

                @block.gpsimd
                def _(g, b=b, mc=mc, n=n, scan0=scan0, ya0=ya0):
                    g.wait_ge(p.sem("v_scan"), scan0 + n + 1)
                    if n > 0:
                        g.wait_ge(p.sem("v_ya"), ya0 + n)
                    p.inc(g.tensor_mul(ytb[:], hbf[:, (n % 2) * SEQ:(n % 2 + 1) * SEQ],
                                       bcb[:, (n % 2) * 2 * SEQ + SEQ:(n % 2 + 1) * 2 * SEQ]),
                          "pl_ym", 1)

            @block.vector
            def _(v, b=b, mc=mc, ym0=ym0):
                v.wait_ge(p.sem("pl_ym"), ym0 + 16)
                p.inc(v.tensor_add(yac, yac, ytb[:]), "v_ya", 1)

            @block.gpsimd
            def _(g, b=b, mc=mc):
                g.wait_ge(p.sem("v_ya"), p.cnt["v_ya"])
                p.inc(g.dma_start(out=y_dram[b, mc * 128:(mc + 1) * 128, :], in_=yac), "st_y", 16)

    @block.gpsimd
    def _(g):
        g.wait_ge(p.sem("mm"), p.cnt["mm"])
        for k in range(16):
            p.inc(g.dma_start(out=wsb[:, k * 1024:(k + 1) * 1024],
                              in_=Wout[k * 128:(k + 1) * 128, :]), "ld_w", 16)

    g6 = 0
    for b in range(BPC):
        for nb4 in range(NB4):
            stg = (b * NB4 + nb4) % 2

            @block.sync
            def _(s, b=b, nb4=nb4, stg=stg, g6=g6):
                if b == 0 and nb4 == 0:
                    s.wait_ge(p.sem("v_scan"), p.cnt["v_scan"])
                    s.wait_ge(p.sem("st_y"), p.cnt["st_y"])
                if g6 >= 2:
                    s.wait_ge(p.sem("v_o"), (g6 - 1) * KD)
                for k in range(KD):
                    p.inc(s.dma_start(out=xnm[:, (stg * 16 + k) * 512:(stg * 16 + k + 1) * 512],
                                      in_=y_dram[b, k * 128:(k + 1) * 128, nb4 * 512:(nb4 + 1) * 512]),
                          "ld_yz", 16)
                for k in range(KD):
                    p.inc(s.dma_start(out=xnm[:, (stg * 16 + 8 + k) * 512:(stg * 16 + 9 + k) * 512],
                                      in_=z_dram[b, k * 128:(k + 1) * 128, nb4 * 512:(nb4 + 1) * 512]),
                          "ld_yz", 16)

            for m in range(KD):
                @block.tensor
                def _(t, b=b, nb4=nb4, m=m, stg=stg, g6=g6):
                    if g6 == 0 and m == 0:
                        t.wait_ge(p.sem("ld_w"), p.cnt["ld_w"])
                        t.wait_ge(p.sem("ev_xz"), p.cnt["ev_xz"])
                    t.wait_ge(p.sem("ld_yz"), p.cnt["ld_yz"])
                    gg = g6 * KD + m if False else p.cnt["p6g"]
                    if gg >= 4:
                        t.wait_ge(p.sem("v_o"), gg - 3)
                    bank = ps[gg % 4]
                    for k in range(16):
                        inst = t.matmul(bank[:], wsb[:, k * 1024 + m * 128:k * 1024 + m * 128 + 128],
                                        xnm[:, (stg * 16 + k) * 512:(stg * 16 + k + 1) * 512],
                                        start=(k == 0), stop=(k == 15))
                    p.inc(inst, "mm", 1)
                    p.cnt["p6g"] = p.cnt.get("p6g", 0) + 1

                @block.sync
                def _(s, b=b, nb4=nb4, m=m, g6=g6):
                    gg = p.cnt["p6g"] - 1
                    if gg == 0:
                        s.wait_ge(p.sem("sc_da"), p.cnt["sc_da"])
                        s.wait_ge(p.sem("v_scan"), p.cnt["v_scan"])
                    if gg >= 2:
                        s.wait_ge(p.sem("v_o"), gg - 1)
                    p.inc(s.dma_start(out=(xres0 if gg % 2 == 0 else xres1),
                                      in_=xin[m * 128:(m + 1) * 128,
                                              b * SEQ + nb4 * 512:b * SEQ + (nb4 + 1) * 512]),
                          "ld_res", 16)

                @block.vector
                def _(v, b=b, nb4=nb4, m=m):
                    gg = p.cnt["p6g"] - 1
                    v.wait_ge(p.sem("mm"), p.cnt["mm"])
                    v.wait_ge(p.sem("ld_res"), p.cnt["ld_res"])
                    if gg >= 2:
                        v.wait_ge(p.sem("st_out"), 16 * (gg - 1))
                    p.inc(v.tensor_add(ob[:, (gg % 2) * 512:(gg % 2 + 1) * 512],
                                       ps[gg % 4][:],
                                       (xres0 if gg % 2 == 0 else xres1)), "v_o", 1)

                @block.gpsimd
                def _(g, b=b, nb4=nb4, m=m):
                    gg = p.cnt["p6g"] - 1
                    g.wait_ge(p.sem("v_o"), p.cnt["v_o"])
                    p.inc(g.dma_start(out=xout[m * 128:(m + 1) * 128,
                                               b * SEQ + nb4 * 512:b * SEQ + (nb4 + 1) * 512],
                                      in_=ob[:, (gg % 2) * 512:(gg % 2 + 1) * 512]), "st_out", 16)
            g6 += 1

    @block.sync
    def _(s):
        s.wait_ge(p.sem("st_out"), p.cnt["st_out"])

    p.close()
    return p.nc


def to_fm_B(x):
    out = []
    for j in range(NCORES):
        xs = x[j * BPC:(j + 1) * BPC, :, :]
        out.append(np.ascontiguousarray(xs.reshape(TOK, D).T))
    return out


def from_fm_B(shards):
    x = np.empty((BATCH, SEQ, D), np.float32)
    for j, s in enumerate(shards):
        x[j * BPC:(j + 1) * BPC] = s.T.reshape(BPC, SEQ, D)
    return x


def run_mamba(x_full, inp, debug=False):
    nc = build_mamba_program(debug)
    shards = to_fm_B(x_full)
    base = {
        "W_in": np.ascontiguousarray(inp["L0_attn_W_in"]),
        "W_xproj": np.ascontiguousarray(inp["L0_attn_W_xproj"]),
        "W_dt": np.ascontiguousarray(inp["L0_attn_W_dt"]),
        "W_out": np.ascontiguousarray(inp["L0_attn_W_out"]),
        "A_log": np.ascontiguousarray(
            inp["L0_attn_A_log"].reshape(KD, 128, 16).transpose(1, 0, 2).reshape(128, KD * 16)).astype(np.float32),
        "K_x": np.ascontiguousarray(
            inp["L0_attn_K_x"].reshape(KD, 128, 4).transpose(1, 0, 2).reshape(128, KD * 4)).astype(np.float32),
        "K_z": np.ascontiguousarray(
            inp["L0_attn_K_z"].reshape(KD, 128, 4).transpose(1, 0, 2).reshape(128, KD * 4)).astype(np.float32),
        "b_dt": vec_chunks(inp["L0_attn_b_dt"]),
        "D_vec": vec_chunks(inp["L0_attn_D"]),
        "b_x": vec_chunks(inp["L0_attn_b_x"]),
        "b_z": vec_chunks(inp["L0_attn_b_z"]),
    }
    in_maps = [dict(base, xin=s) for s in shards]
    res = run_bass_kernel_spmd(nc, in_maps, core_ids=list(range(NCORES)))
    out = from_fm_B([res.results[j]["xout"] for j in range(NCORES)])
    if debug:
        return out, res.results
    return out




def build_moe_program(debug=False):
    p = Prog("moe")
    nc = p.nc
    xin = nc.dram_tensor("xin", [D, TOK], F32, kind="ExternalInput").ap()
    GW = nc.dram_tensor("gating_w", [128, KD * 12], F32, kind="ExternalInput").ap()
    gate_w = nc.dram_tensor("gate_w", [8, D, 2048], F32, kind="ExternalInput").ap()
    up_w = nc.dram_tensor("up_w", [8, D, 2048], F32, kind="ExternalInput").ap()
    down_w = nc.dram_tensor("down_w", [8, 2048, D], F32, kind="ExternalInput").ap()
    xout = nc.dram_tensor("xout", [D, TOK], F32, kind="ExternalOutput").ap()
    g_bf = nc.dram_tensor("g_bf", [8, D, 2048], BF16, kind="Internal").ap()
    u_bf = nc.dram_tensor("u_bf", [8, D, 2048], BF16, kind="Internal").ap()
    d_bf = nc.dram_tensor("d_bf", [8, 2048, D], BF16, kind="Internal").ap()
    w_dram = nc.dram_tensor("w_dram", [8, TOK], BF16,
                            kind="ExternalOutput" if debug else "Internal").ap()
    if debug:
        dbg_wbc = nc.dram_tensor("dbg_wbc", [128, 2048], BF16, kind="ExternalOutput").ap()
        dbg_hh = nc.dram_tensor("dbg_hh", [128, 8 * 2048], BF16, kind="ExternalOutput").ap()
        dbg_yf = nc.dram_tensor("dbg_yf", [128, 8 * 2048], BF16, kind="ExternalOutput").ap()

    HTOK = 2048
    xnm = p.sbuf("xnm", [128, KD * HTOK], BF16)
    yffn = p.sbuf("yffn", [128, KD * HTOK], BF16)
    hh = p.sbuf("hh", [128, 8 * HTOK], BF16)
    mB = p.sbuf("mB", [128, TOK], BF16)
    rB = p.sbuf("rB", [128, TOK], BF16)
    xsb = p.sbuf("xsb", [128, KD * 512], F32)
    dwt = p.sbuf("dwt", [128, 8 * 1024], BF16)
    gwt = p.sbuf("gwt", [128, 2 * KD * 128], BF16)
    uwt = p.sbuf("uwt", [128, 2 * KD * 128], BF16)
    wbc = p.sbuf("wbc", [128, HTOK], BF16)
    gw_sb = p.sbuf("gw_sb", [128, KD * 12], BF16)
    gsc = p.sbuf("gsc", [128, 12], F32)
    wex = p.sbuf("wex", [128, 8], F32)
    mx = p.sbuf("mx", [128, 1], F32)
    u1 = p.sbuf("u1", [128, 4], F32)
    u2 = p.sbuf("u2", [128, 4], F32)
    msum = p.sbuf("msum", [128, 4], F32)
    maskg = p.sbuf("maskg", [128, 4], F32)
    wfac = p.sbuf("wfac", [128, 4], F32)
    wmac = p.sbuf("wmac", [128, 1], F32)
    identf = p.sbuf("identf", [128, 128], F32)
    epst = p.sbuf("epst", [128, 4], F32)
    ssc = p.sbuf("ssc", [128, 2 * 512], BF16)
    usc = p.sbuf("usc", [128, 2 * 512], BF16)
    ob = p.sbuf("ob", [128, 2 * 512], F32)
    ones_col = p.sbuf("ones_col", [128, 1], BF16)
    ones_row = p.sbuf("ones_row", [1, 128], BF16)
    epsr = p.sbuf("epsr", [128, 1], F32)
    assert p._sbuf_bytes <= 226000, p._sbuf_bytes

    rowA = ob[0:1, 0:512]
    rowB = ob[0:1, 512:1024]
    rowC = usc[:].bitcast(F32)[0:1, 0:512]
    rowMb = wbc[0:1, 0:512]
    rowRb = wbc[0:1, 512:1024]
    xbb = hh[:, 0:KD * 512]
    wrow = yffn[:].bitcast(F32)[0:8, 0:HTOK]

    ps = [p.psum(f"ps{i}", [128, 512], F32) for i in range(8)]

    p.start_block()
    block = p.block
    for nm in ("mm", "ld_x", "ld_v", "v_xn", "v_gt", "ev_gt", "mm_gt", "v_wx", "v_wr",
               "ld_gu", "ld_d", "v_up", "v_h", "v_dn", "st_w", "ld_wb",
               "ld_res", "v_o", "st_out", "cast_w", "v_mb") + tuple(f"evb{i}" for i in range(8)):
        p.sem(nm)
    p.cnt["mgrp"] = 0
    bank_cnt = [0] * 8

    def open_group(t_, nbank):
        if bank_cnt[nbank] > 0:
            t_.wait_ge(p.sem(f"evb{nbank}"), bank_cnt[nbank])
        bank_cnt[nbank] += 1
        return ps[nbank]

    @block.gpsimd
    def _(g):
        p.inc(g.dma_start(out=gw_sb[:], in_=GW[:]), "ld_v", 16)
        g.memset(ones_col[:], 1.0)
        g.memset(ones_row[:], 1.0)
        g.memset(epsr[:], 1e-5)
        g.memset(epst[:], 1e-9)
        g.memset(identf[:], 0.0)
        p.inc(g.affine_select(out=identf[:], in_=identf[:], compare_op=ALU.not_equal,
                              fill=1.0, base=0, pattern=[[-1, 128]], channel_multiplier=1),
              "ld_v", 1)
        for e in range(8):
            p.inc(g.dma_start(out=g_bf[e], in_=gate_w[e]), "cast_w", 16)
            p.inc(g.dma_start(out=u_bf[e], in_=up_w[e]), "cast_w", 16)
            p.inc(g.dma_start(out=d_bf[e], in_=down_w[e]), "cast_w", 16)

    emit_ln_stats(p, block, ps, xin, mB, rB, TOK,
                  dict(xsb=xsb[:], xbb=xbb, sqs=ssc[:],
                       ones_col=ones_col, ones_row=ones_row, epsr=epsr, rowA=rowA,
                       rowB=rowB, rowC=rowC, rowMb=rowMb, rowRb=rowRb))

    mm_prev_half = 0
    for th in range(2):
        tc0 = th * HTOK

        for k in range(KD):
            half = k % 2

            @block.sync
            def _(s, th=th, k=k, half=half, tc0=tc0):
                if th == 0 and k == 0:
                    s.wait_ge(p.sem("ln_sq"), KD * (TOK // 512))
                    s.wait_ge(p.sem("ln_cast"), TOK // 512)
                if th == 1 and k == 0:
                    s.wait_ge(p.sem("v_o"), 32)
                if th * KD + k >= 2:
                    s.wait_ge(p.sem("v_xn"), th * KD + k - 1)
                p.inc(s.dma_start(out=xsb[:, half * 2048:(half + 1) * 2048],
                                  in_=xin[k * 128:(k + 1) * 128, tc0:tc0 + HTOK]), "ld_x", 16)

            @block.vector
            def _(v, th=th, k=k, half=half, tc0=tc0):
                v.wait_ge(p.sem("ld_x"), p.cnt["ld_x"])
                if th == 0 and k == 0:
                    v.wait_ge(p.sem("v_mb"), TOK // 512)
                if th == 1 and k == 0:
                    v.wait_ge(p.sem("mm"), mm_prev_half)
                v.tensor_sub(xsb[:, half * 2048:(half + 1) * 2048],
                             xsb[:, half * 2048:(half + 1) * 2048], mB[:, tc0:tc0 + HTOK])
                p.inc(v.tensor_mul(xnm[:, k * HTOK:(k + 1) * HTOK],
                                   xsb[:, half * 2048:(half + 1) * 2048], rB[:, tc0:tc0 + HTOK]),
                      "v_xn", 1)

        for tt in range(16):
            @block.tensor
            def _(t_, th=th, tt=tt):
                t_.wait_ge(p.sem("v_xn"), KD * (th + 1))
                if th == 0 and tt == 0:
                    t_.wait_ge(p.sem("ld_v"), p.cnt["ld_v"])
                    t_.wait_ge(p.sem("v_mb"), TOK // 512)
                bank = open_group(t_, 4)
                for k in range(KD):
                    inst = t_.matmul(bank[:, 0:12], xnm[:, k * HTOK + tt * 128:k * HTOK + (tt + 1) * 128],
                                     gw_sb[:, k * 12:(k + 1) * 12],
                                     start=(k == 0), stop=(k == KD - 1))
                p.inc(inst, "mm_gt", 1)

            @block.scalar
            def _(s, th=th, tt=tt):
                s.wait_ge(p.sem("mm_gt"), p.cnt["mm_gt"])
                if th > 0 or tt > 0:
                    s.wait_ge(p.sem("v_wx"), p.cnt["v_wx"])
                p.inc(s.activation(gsc[:], ps[4][:, 0:12], AF.Sigmoid), "ev_gt", 1)
                p.inc(s.engine_nop() if False else s.copy(gsc[0:1, 0:1], gsc[0:1, 0:1]), f"evb4", 1)

            @block.vector
            def _(v, th=th, tt=tt):
                def step(op):
                    p.inc(op, "v_gt", 1)
                    v.wait_ge(p.sem("v_gt"), p.cnt["v_gt"])
                v.wait_ge(p.sem("ev_gt"), p.cnt["ev_gt"])
                gt = th * 16 + tt
                if gt > 0:
                    v.wait_ge(p.sem("mm_gt"), 2 * gt)
                step(v.reduce_max(mx[:], gsc[:, 0:4], axis=AX.X))
                step(v.tensor_scalar(maskg[:], gsc[:, 0:4], mx[:, 0:1], None, ALU.is_equal))
                step(v.tensor_add(msum[:], gsc[:, 4:12].rearrange("q (g e) -> q g e", e=2)[:, :, 0],
                                  gsc[:, 4:12].rearrange("q (g e) -> q g e", e=2)[:, :, 1]))
                step(v.tensor_add(u1[:], msum[:], epst[:]))
                step(v.reciprocal(u2[:], u1[:]))
                step(v.tensor_add(u1[:, 0:1], mx[:], epst[:, 0:1]))
                step(v.reciprocal(u1[:, 1:2], u1[:, 0:1]))
                step(v.tensor_mul(wmac[:], mx[:], u1[:, 1:2]))
                step(v.tensor_mul(wfac[:], maskg[:], u2[:]))
                step(v.tensor_scalar(wfac[:], wfac[:], wmac[:, 0:1], None, ALU.mult))
                p.inc(v.tensor_mul(wex[:].rearrange("q (g o) -> q g o", o=2),
                                   gsc[:, 4:12].rearrange("q (g o) -> q g o", o=2),
                                   wfac[:].rearrange("q (g o) -> q g o", o=1)
                                   .broadcast_to([128, 4, 2])),
                      "v_wx", 1)

            @block.tensor
            def _(t_, th=th, tt=tt):
                t_.wait_ge(p.sem("v_wx"), p.cnt["v_wx"])
                bank = open_group(t_, 5)
                p.inc(t_.transpose(bank[0:8, 0:128], wex[:], identf[:]), "mm_gt", 1)

            @block.vector
            def _(v, th=th, tt=tt):
                v.wait_ge(p.sem("mm_gt"), p.cnt["mm_gt"])
                p.inc(v.tensor_copy(wrow[0:8, tt * 128:(tt + 1) * 128], ps[5][0:8, 0:128]),
                      "v_wr", 1)
                p.inc(v.engine_nop(), f"evb5", 1)

        @block.gpsimd
        def _(g, th=th, tc0=tc0):
            g.wait_ge(p.sem("v_wr"), p.cnt["v_wr"])
            p.inc(g.dma_start(out=w_dram[:, tc0:tc0 + HTOK], in_=wrow[:]), "st_w", 16)

        for e in range(8):
            @block.sync
            def _(s, th=th, e=e, tc0=tc0):
                s.wait_ge(p.sem("st_w"), 16 * (th + 1))
                if th > 0 or e > 0:
                    s.wait_ge(p.sem("v_up"), p.cnt["v_up"])
                p.inc(s.dma_start(
                    out=wbc[:].rearrange("q (r c) -> q r c", r=1).broadcast_to([128, 1, HTOK])
                    if False else wbc[:],
                    in_=w_dram[e:e + 1, tc0:tc0 + HTOK].partition_broadcast(128)[:, 0, :]
                    if False else w_dram[e:e + 1, tc0:tc0 + HTOK].partition_broadcast(128)),
                    "ld_wb", 16)

            for fh in range(2):
                @block.scalar
                def _(s, th=th, e=e, fh=fh):
                    s.wait_ge(p.sem("cast_w"), 16 * (3 * e + 3))
                    s.wait_ge(p.sem("mm"), p.cnt["mm"])
                    for fc8 in range(8):
                        p.inc(s.dma_start(out=dwt[:, fc8 * 1024:(fc8 + 1) * 1024],
                                          in_=d_bf[e, (fh * 8 + fc8) * 128:(fh * 8 + fc8 + 1) * 128, :]),
                              "ld_d", 16)

                for fc8 in range(8):
                    fc = fh * 8 + fc8
                    par = fc % 2

                    @block.sync
                    def _(s, th=th, e=e, fc=fc, fc8=fc8, par=par):
                        s.wait_ge(p.sem("cast_w"), 16 * (3 * e + 3))
                        gi = ((th * 8 + e) * 2 + fh) * 8 + fc8
                        if gi >= 2:
                            s.wait_ge(p.sem("mm"), p.cnt.get(f"_gu_mm_{gi - 2}", 0))
                        p.inc(s.dma_start(out=gwt[:, par * 1024:(par + 1) * 1024]
                                          .rearrange("q (k c) -> q k c", k=KD),
                                          in_=g_bf[e].rearrange("(k q) f -> q k f", q=128)
                                          [:, :, fc * 128:(fc + 1) * 128]), "ld_gu", 16)
                        p.inc(s.dma_start(out=uwt[:, par * 1024:(par + 1) * 1024]
                                          .rearrange("q (k c) -> q k c", k=KD),
                                          in_=u_bf[e].rearrange("(k q) f -> q k f", q=128)
                                          [:, :, fc * 128:(fc + 1) * 128]), "ld_gu", 16)

                    @block.tensor
                    def _(t_, th=th, e=e, fc=fc, fc8=fc8, fh=fh, par=par):
                        t_.wait_ge(p.sem("ld_gu"), p.cnt["ld_gu"])
                        if th == 0 and e == 0 and fc == 0:
                            t_.wait_ge(p.sem("v_wr"), p.cnt["v_wr"])
                        for tb in range(4):
                            bank = open_group(t_, tb % 4)
                            for k in range(KD):
                                inst = t_.matmul(bank[:], gwt[:, par * 1024 + k * 128:par * 1024 + (k + 1) * 128],
                                                 xnm[:, k * HTOK + tb * 512:k * HTOK + (tb + 1) * 512],
                                                 start=(k == 0), stop=(k == KD - 1))
                            p.inc(inst, "mm", 1)
                        for tb in range(4):
                            bank = open_group(t_, 4 + tb % 4)
                            for k in range(KD):
                                inst = t_.matmul(bank[:], uwt[:, par * 1024 + k * 128:par * 1024 + (k + 1) * 128],
                                                 xnm[:, k * HTOK + tb * 512:k * HTOK + (tb + 1) * 512],
                                                 start=(k == 0), stop=(k == KD - 1))
                            p.inc(inst, "mm", 1)
                        p.cnt[f"_gu_mm_{((th * 8 + e) * 2 + fh) * 8 + fc8}"] = p.cnt["mm"]

                    @block.scalar
                    def _(s, th=th, e=e, fc=fc):
                        mmb = p.cnt["mm"] - 8
                        vhb = p.cnt["v_h"]
                        for tb in range(4):
                            s.wait_ge(p.sem("mm"), mmb + tb + 1)
                            if vhb + tb - 1 >= 1:
                                s.wait_ge(p.sem("v_h"), vhb + tb - 1)
                            p.inc(s.activation(ssc[:, (tb % 2) * 512:(tb % 2 + 1) * 512],
                                               ps[tb % 4][:], AF.Silu), f"evb{tb % 4}", 1)

                    @block.vector
                    def _(v, th=th, e=e, fc=fc, fc8=fc8):
                        mmb = p.cnt["mm"] - 8
                        v.wait_ge(p.sem("ld_wb"), p.cnt["ld_wb"])
                        for tb in range(4):
                            v.wait_ge(p.sem("mm"), mmb + 4 + tb + 1)
                            v.tensor_mul(usc[:, (tb % 2) * 512:(tb % 2 + 1) * 512],
                                         ps[4 + tb % 4][:], wbc[:, tb * 512:(tb + 1) * 512])
                            p.inc(v.engine_nop(), f"evb{4 + tb % 4}", 1)
                            v.wait_ge(p.sem(f"evb{tb % 4}"), bank_cnt[tb % 4])
                            p.inc(v.tensor_mul(hh[:, fc8 * HTOK + tb * 512:fc8 * HTOK + (tb + 1) * 512],
                                               ssc[:, (tb % 2) * 512:(tb % 2 + 1) * 512],
                                               usc[:, (tb % 2) * 512:(tb % 2 + 1) * 512]), "v_h", 1)
                        p.inc(v.engine_nop(), "v_up", 1)

                @block.tensor
                def _(t_, th=th, e=e, fh=fh):
                    t_.wait_ge(p.sem("v_h"), p.cnt["v_h"])
                    t_.wait_ge(p.sem("ld_d"), p.cnt["ld_d"])
                    for md in range(KD):
                        for tb in range(4):
                            bank = open_group(t_, (md * 4 + tb) % 8)
                            for fc8 in range(8):
                                inst = t_.matmul(bank[:], dwt[:, fc8 * 1024 + md * 128:fc8 * 1024 + (md + 1) * 128],
                                                 hh[:, fc8 * HTOK + tb * 512:fc8 * HTOK + (tb + 1) * 512],
                                                 start=(fc8 == 0), stop=(fc8 == 7))
                            p.inc(inst, "mm", 1)

                @block.vector
                def _(v, th=th, e=e, fh=fh):
                    mmb = p.cnt["mm"] - 32
                    for md in range(KD):
                        for tb in range(4):
                            gidx = md * 4 + tb
                            v.wait_ge(p.sem("mm"), mmb + gidx + 1)
                            sl = slice(md * HTOK + tb * 512, md * HTOK + (tb + 1) * 512)
                            if e == 0 and fh == 0:
                                v.tensor_copy(yffn[:, sl], ps[gidx % 8][:])
                            else:
                                v.tensor_add(yffn[:, sl], yffn[:, sl], ps[gidx % 8][:])
                            p.inc(v.engine_nop(), f"evb{gidx % 8}", 1)
                    p.inc(v.engine_nop(), "v_dn", 1)

        for md in range(KD):
            for tb in range(4):
                gg = th * 32 + md * 4 + tb

                @block.sync
                def _(s, th=th, md=md, tb=tb, gg=gg, tc0=tc0):
                    s.wait_ge(p.sem("v_dn"), 16 * (th + 1))
                    if gg >= 2:
                        s.wait_ge(p.sem("v_o"), gg - 1)
                    p.inc(s.dma_start(out=xsb[:, (gg % 2) * 512:(gg % 2 + 1) * 512],
                                      in_=xin[md * 128:(md + 1) * 128,
                                              tc0 + tb * 512:tc0 + (tb + 1) * 512]), "ld_res", 16)

                @block.vector
                def _(v, th=th, md=md, tb=tb, gg=gg):
                    v.wait_ge(p.sem("ld_res"), p.cnt["ld_res"])
                    v.wait_ge(p.sem("v_dn"), 16 * (th + 1))
                    if gg >= 2:
                        v.wait_ge(p.sem("st_out"), 16 * (gg - 1))
                    p.inc(v.scalar_tensor_tensor(ob[:, (gg % 2) * 512:(gg % 2 + 1) * 512],
                                                 yffn[:, md * HTOK + tb * 512:md * HTOK + (tb + 1) * 512],
                                                 0.5, xsb[:, (gg % 2) * 512:(gg % 2 + 1) * 512],
                                                 ALU.mult, ALU.add), "v_o", 1)

                @block.gpsimd
                def _(g, th=th, md=md, tb=tb, gg=gg, tc0=tc0):
                    g.wait_ge(p.sem("v_o"), p.cnt["v_o"])
                    p.inc(g.dma_start(out=xout[md * 128:(md + 1) * 128,
                                               tc0 + tb * 512:tc0 + (tb + 1) * 512],
                                      in_=ob[:, (gg % 2) * 512:(gg % 2 + 1) * 512]), "st_out", 16)

        mm_prev_half = p.cnt["mm"]

    if debug:
        @block.gpsimd
        def _(g):
            g.wait_ge(p.sem("v_up"), p.cnt["v_up"])
            p.inc(g.dma_start(out=dbg_wbc[:], in_=wbc[:]), "st_out", 16)
            g.wait_ge(p.sem("v_h"), p.cnt["v_h"])
            p.inc(g.dma_start(out=dbg_hh[:], in_=hh[:]), "st_out", 16)
            g.wait_ge(p.sem("v_dn"), p.cnt["v_dn"])
            p.inc(g.dma_start(out=dbg_yf[:], in_=yffn[:]), "st_out", 16)

    @block.sync
    def _(s):
        s.wait_ge(p.sem("st_out"), p.cnt["st_out"])

    p.close()
    return p.nc


def moe_host_inputs(inp, prefix="L1_ffn"):
    macro = np.asarray(inp[f"{prefix}_macro_w"], np.float32)
    micro = np.asarray(inp[f"{prefix}_micro_w"], np.float32)
    gw = np.empty((128, KD * 12), np.float32)
    for k in range(KD):
        gw[:, k * 12:k * 12 + 4] = macro[k * 128:(k + 1) * 128, :]
        mic = micro[:, k * 128:(k + 1) * 128, :]
        gw[:, k * 12 + 4:k * 12 + 12] = mic.transpose(1, 0, 2).reshape(128, 8)
    return {
        "gating_w": np.ascontiguousarray(gw),
        "gate_w": np.ascontiguousarray(inp[f"{prefix}_gate_w"]),
        "up_w": np.ascontiguousarray(inp[f"{prefix}_up_w"]),
        "down_w": np.ascontiguousarray(inp[f"{prefix}_down_w"]),
    }


def run_moe(x_full, inp, pshard=True, debug=False):
    nc = build_moe_program(debug)
    shards = to_fm_P(x_full) if pshard else to_fm_B(x_full)
    base = moe_host_inputs(inp)
    in_maps = [dict(base, xin=s) for s in shards]
    res = run_bass_kernel_spmd(nc, in_maps, core_ids=list(range(NCORES)))
    outs = [res.results[j]["xout"] for j in range(NCORES)]
    full = from_fm_P(outs) if pshard else from_fm_B(outs)
    if debug:
        return full, res.results
    return full




def build_gru_program():
    p = Prog("gru")
    nc = p.nc
    CC = 12
    xin = nc.dram_tensor("xin", [D, TOK], F32, kind="ExternalInput").ap()
    Whg = nc.dram_tensor("W_hg", [2 * 1536, D], F32, kind="ExternalInput").ap()
    Wo1 = nc.dram_tensor("W_out1", [D, 1536], F32, kind="ExternalInput").ap()
    BHG = nc.dram_tensor("b_hg", [128, 2 * CC], F32, kind="ExternalInput").ap()
    BO = nc.dram_tensor("b_out", [128, KD], F32, kind="ExternalInput").ap()
    H0 = nc.dram_tensor("h0", [128, CC], F32, kind="ExternalInput").ap()
    xout = nc.dram_tensor("xout", [D, TOK], F32, kind="ExternalOutput").ap()
    whg_bf = nc.dram_tensor("whg_bf", [2 * 1536, D], BF16, kind="Internal").ap()
    wo1_bf = nc.dram_tensor("wo1_bf", [D, 1536], BF16, kind="Internal").ap()

    mB = p.sbuf("mB", [128, TOK], BF16)
    rB = p.sbuf("rB", [128, TOK], BF16)
    xsb = p.sbuf("xsb", [128, KD * 512], F32)
    xbb = p.sbuf("xbb", [128, KD * 512], BF16)
    xnm = p.sbuf("xnm", [128, KD * SEQ], BF16)
    hbu = p.sbuf("hbu", [128, CC * SEQ], BF16)
    wpr = p.sbuf("wpr", [128, 2 * 16 * 128], BF16)
    wot = p.sbuf("wot", [128, 2 * CC * 128], BF16)
    a_t = p.sbuf("a_t", [128, SEQ], F32)
    b_t = p.sbuf("b_t", [128, SEQ], F32)
    zfu = p.sbuf("zfu", [128, SEQ], F32)
    ghf = p.sbuf("ghf", [128, SEQ], F32)
    hp5 = p.sbuf("hp5", [128, 512], F32)
    sgs = p.sbuf("sgs", [128, 512], F32)
    msk = p.sbuf("msk", [128, 512], U8)
    ob = p.sbuf("ob", [128, 2 * 512], F32)
    bhv = p.sbuf("bhv", [128, 2 * CC], F32)
    nbg = p.sbuf("nbg", [128, CC], F32)
    bh5 = p.sbuf("bh5", [128, CC], F32)
    nbh = p.sbuf("nbh", [128, CC], F32)
    h0v = p.sbuf("h0v", [128, CC], F32)
    ghz = p.sbuf("ghz", [128, CC], F32)
    bov = p.sbuf("bov", [128, KD], F32)
    mh0 = p.sbuf("mh0", [128, CC], U8)
    sh0 = p.sbuf("sh0", [128, CC], F32)
    ones_col = p.sbuf("ones_col", [128, 1], BF16)
    ones_row = p.sbuf("ones_row", [1, 128], BF16)
    epsr = p.sbuf("epsr", [128, 1], F32)
    rowA = p.sbuf("rowA", [1, 512], F32)
    rowB = p.sbuf("rowB", [1, 512], F32)
    rowC = p.sbuf("rowC", [1, 512], F32)
    rowMb = p.sbuf("rowMb", [1, 512], BF16)
    rowRb = p.sbuf("rowRb", [1, 512], BF16)
    sqs = hbu[:, 0:1024]
    assert p._sbuf_bytes <= 226000, p._sbuf_bytes

    ps = [p.psum(f"ps{i}", [128, 512], F32) for i in range(8)]

    p.start_block()
    block = p.block
    for nm in ("mm", "ld_x", "ld_v", "v_xn", "cast_w", "ld_wp", "ld_wo", "v_ab",
               "sc_ab", "v_scan", "ld_res", "v_o", "st_out", "v_mb", "v_pre") + \
            tuple(f"evb{i}" for i in range(8)):
        p.sem(nm)
    bank_cnt = [0] * 8

    def open_group(t_, nbank):
        if bank_cnt[nbank] > 0:
            t_.wait_ge(p.sem(f"evb{nbank}"), bank_cnt[nbank])
        bank_cnt[nbank] += 1
        return ps[nbank]

    @block.gpsimd
    def _(g):
        p.inc(g.dma_start(out=whg_bf[:], in_=Whg[:]), "cast_w", 16)
        p.inc(g.dma_start(out=wo1_bf[:], in_=Wo1[:]), "cast_w", 16)
        p.inc(g.dma_start(out=bhv[:], in_=BHG[:]), "ld_v", 16)
        p.inc(g.dma_start(out=h0v[:], in_=H0[:]), "ld_v", 16)
        p.inc(g.dma_start(out=bov[:], in_=BO[:]), "ld_v", 16)
        g.memset(ones_col[:], 1.0)
        g.memset(ones_row[:], 1.0)
        p.inc(g.memset(epsr[:], 1e-5), "ld_v", 1)

    @block.scalar
    def _(s):
        s.wait_ge(p.sem("ld_v"), p.cnt["ld_v"])
        p.inc(s.activation(sh0[:], h0v[:], AF.Sigmoid), "sc_ab", 1)

    @block.vector
    def _(v):
        def step(op):
            p.inc(op, "v_pre", 1)
            v.wait_ge(p.sem("v_pre"), p.cnt["v_pre"])
        v.wait_ge(p.sem("ld_v"), p.cnt["ld_v"])
        v.wait_ge(p.sem("sc_ab"), p.cnt["sc_ab"])
        step(v.tensor_scalar(nbg[:], bhv[:, CC:2 * CC], -1.0, None, ALU.mult))
        step(v.tensor_scalar(bh5[:], bhv[:, 0:CC], 0.5, None, ALU.add))
        step(v.tensor_scalar(nbh[:], bhv[:, 0:CC], -1.0, None, ALU.mult))
        step(v.tensor_scalar(msk[:, 0:CC].bitcast(U8) if False else mh0[:],
                             h0v[:], 0.0, None, ALU.is_ge))
        step(v.tensor_scalar(ghz[:], h0v[:], 0.5, None, ALU.add))
        step(v.select(ghz[:], mh0[:], ghz[:], sh0[:]))

    emit_ln_stats(p, block, ps, xin, mB, rB, TOK,
                  dict(xsb=xsb[:], xbb=xbb[:], sqs=sqs,
                       ones_col=ones_col, ones_row=ones_row, epsr=epsr, rowA=rowA,
                       rowB=rowB, rowC=rowC, rowMb=rowMb, rowRb=rowRb))

    NB4 = SEQ // 512
    mm_hg_end = {}

    for b in range(BPC):
        c0 = b * SEQ

        for k in range(KD):
            half = k % 2

            @block.sync
            def _(s, b=b, k=k, half=half, c0=c0):
                if b == 0 and k == 0:
                    s.wait_ge(p.sem("ln_sq"), KD * (TOK // 512))
                    s.wait_ge(p.sem("ln_cast"), TOK // 512)
                if b == 1 and k == 0:
                    s.wait_ge(p.sem("v_o"), 32)
                if b * KD + k >= 2:
                    s.wait_ge(p.sem("v_xn"), b * KD + k - 1)
                p.inc(s.dma_start(out=xsb[:, half * SEQ:(half + 1) * SEQ],
                                  in_=xin[k * 128:(k + 1) * 128, c0:c0 + SEQ]), "ld_x", 16)

            @block.vector
            def _(v, b=b, k=k, half=half, c0=c0):
                v.wait_ge(p.sem("ld_x"), p.cnt["ld_x"])
                if b == 0 and k == 0:
                    v.wait_ge(p.sem("v_mb"), TOK // 512)
                if b == 1 and k == 0:
                    v.wait_ge(p.sem("mm"), mm_hg_end[0])
                v.tensor_sub(xsb[:, half * SEQ:(half + 1) * SEQ],
                             xsb[:, half * SEQ:(half + 1) * SEQ], mB[:, c0:c0 + SEQ])
                p.inc(v.tensor_mul(xnm[:, k * SEQ:(k + 1) * SEQ],
                                   xsb[:, half * SEQ:(half + 1) * SEQ], rB[:, c0:c0 + SEQ]),
                      "v_xn", 1)

        for cc in range(CC):
            par = cc % 2
            gl = b * CC + cc

            @block.scalar
            def _(s, b=b, cc=cc, par=par, gl=gl):
                s.wait_ge(p.sem("cast_w"), 16)
                if gl >= 2:
                    s.wait_ge(p.sem("mm"), p.cnt.get(f"_hg_mm_{gl - 2}", 0))
                for k in range(KD):
                    p.inc(s.dma_start(out=wpr[:, par * 2048 + k * 128:par * 2048 + (k + 1) * 128],
                                      in_=whg_bf[cc * 128:(cc + 1) * 128, k * 128:(k + 1) * 128],
                                      transpose=True), "ld_wp", 16)
                for k in range(KD):
                    p.inc(s.dma_start(out=wpr[:, par * 2048 + 1024 + k * 128:par * 2048 + 1024 + (k + 1) * 128],
                                      in_=whg_bf[(CC + cc) * 128:(CC + cc + 1) * 128, k * 128:(k + 1) * 128],
                                      transpose=True), "ld_wp", 16)

            @block.tensor
            def _(t_, b=b, cc=cc, par=par, gl=gl):
                t_.wait_ge(p.sem("ld_wp"), p.cnt["ld_wp"])
                t_.wait_ge(p.sem("v_xn"), KD * (b + 1))
                if b == 0 and cc == 0:
                    t_.wait_ge(p.sem("v_mb"), TOK // 512)
                for nb4 in range(NB4):
                    bank = open_group(t_, nb4)
                    for k in range(KD):
                        inst = t_.matmul(bank[:], wpr[:, par * 2048 + k * 128:par * 2048 + (k + 1) * 128],
                                         xnm[:, k * SEQ + nb4 * 512:k * SEQ + (nb4 + 1) * 512],
                                         start=(k == 0), stop=(k == KD - 1))
                    p.inc(inst, "mm", 1)
                for nb4 in range(NB4):
                    bank = open_group(t_, 4 + nb4)
                    for k in range(KD):
                        inst = t_.matmul(bank[:], wpr[:, par * 2048 + 1024 + k * 128:par * 2048 + 1024 + (k + 1) * 128],
                                         xnm[:, k * SEQ + nb4 * 512:k * SEQ + (nb4 + 1) * 512],
                                         start=(k == 0), stop=(k == KD - 1))
                    p.inc(inst, "mm", 1)
                p.cnt[f"_hg_mm_{gl}"] = p.cnt["mm"]

            @block.scalar
            def _(s, b=b, cc=cc, gl=gl):
                mmb = p.cnt["mm"] - 8
                vab_base = p.cnt["v_ab"]
                if gl >= 1:
                    s.wait_ge(p.sem("v_scan"), gl)
                for nb4 in range(NB4):
                    sl = slice(nb4 * 512, (nb4 + 1) * 512)
                    s.wait_ge(p.sem("mm"), mmb + nb4 + 1)
                    if vab_base + nb4 >= 1:
                        s.wait_ge(p.sem("v_ab"), vab_base + nb4)
                    s.activation(sgs[:], ps[nb4][:], AF.Sigmoid, bias=bhv[:, cc:cc + 1])
                    p.inc(s.add(hp5[:], ps[nb4][:], bh5[:, cc:cc + 1]), "sc_ab", 1)
                for nb4 in range(NB4):
                    sl = slice(nb4 * 512, (nb4 + 1) * 512)
                    s.wait_ge(p.sem("mm"), mmb + 4 + nb4 + 1)
                    s.activation(a_t[:, sl], ps[4 + nb4][:], AF.Sigmoid,
                                 bias=nbg[:, cc:cc + 1], scale=-1.0)
                    p.inc(s.activation(zfu[:, sl], ps[4 + nb4][:], AF.Sigmoid,
                                       bias=bhv[:, CC + cc:CC + cc + 1]), "sc_ab", 1)

            @block.vector
            def _(v, b=b, cc=cc, gl=gl):
                scb = p.cnt["sc_ab"] - 8
                if b == 1 and cc == 0:
                    v.wait_ge(p.sem("mm"), p.cnt["mm"])
                for nb4 in range(NB4):
                    sl = slice(nb4 * 512, (nb4 + 1) * 512)
                    v.wait_ge(p.sem("sc_ab"), scb + nb4 + 1)
                    v.tensor_scalar(msk[:], ps[nb4][:], nbh[:, cc:cc + 1], None, ALU.is_ge)
                    p.inc(v.select(ghf[:, sl], msk[:], hp5[:], sgs[:]), "v_ab", 1)
                    p.inc(v.engine_nop(), f"evb{nb4}", 1)
                v.wait_ge(p.sem("sc_ab"), scb + 8)
                for nb4 in range(NB4):
                    p.inc(v.engine_nop(), f"evb{4 + nb4}", 1)
                p.inc(v.tensor_mul(b_t[:], zfu[:], ghf[:]), "v_ab", 1)
                if gl >= 2:
                    pass
                p.inc(v.tensor_tensor_scan(hbu[:, cc * SEQ:(cc + 1) * SEQ],
                                           a_t[:], b_t[:], ghz[:, cc:cc + 1],
                                           ALU.mult, ALU.add), "v_scan", 1)

        mm_hg_end[b] = p.cnt["mm"]

        for md in range(KD):
            par = md % 2

            @block.scalar
            def _(s, b=b, md=md, par=par):
                s.wait_ge(p.sem("cast_w"), 32)
                gl = b * KD + md
                if gl >= 2:
                    s.wait_ge(p.sem("mm"), p.cnt.get(f"_ou_mm_{gl - 2}", 0))
                for k in range(CC):
                    p.inc(s.dma_start(out=wot[:, par * 1536 + k * 128:par * 1536 + (k + 1) * 128],
                                      in_=wo1_bf[md * 128:(md + 1) * 128, k * 128:(k + 1) * 128],
                                      transpose=True), "ld_wo", 16)

            for nb4 in range(NB4):
                @block.tensor
                def _(t_, b=b, md=md, nb4=nb4, par=par):
                    t_.wait_ge(p.sem("ld_wo"), p.cnt["ld_wo"])
                    t_.wait_ge(p.sem("v_scan"), CC * (b + 1))
                    gg = p.cnt.get("p6g", 0)
                    bank = open_group(t_, gg % 4)
                    for k in range(CC):
                        inst = t_.matmul(bank[:], wot[:, par * 1536 + k * 128:par * 1536 + (k + 1) * 128],
                                         hbu[:, k * SEQ + nb4 * 512:k * SEQ + (nb4 + 1) * 512],
                                         start=(k == 0), stop=(k == CC - 1))
                    p.inc(inst, "mm", 1)
                    p.cnt["p6g"] = gg + 1
                    p.cnt[f"_ou_mm_{b * KD + md}"] = p.cnt["mm"]

                @block.sync
                def _(s, b=b, md=md, nb4=nb4, c0=c0):
                    gg = p.cnt["p6g"] - 1
                    s.wait_ge(p.sem("v_xn"), KD * (b + 1))
                    if gg >= 2:
                        s.wait_ge(p.sem("v_o"), gg - 1)
                    p.inc(s.dma_start(out=xsb[:, (gg % 2) * 512:(gg % 2 + 1) * 512],
                                      in_=xin[md * 128:(md + 1) * 128,
                                              c0 + nb4 * 512:c0 + (nb4 + 1) * 512]), "ld_res", 16)

                @block.vector
                def _(v, b=b, md=md, nb4=nb4):
                    gg = p.cnt["p6g"] - 1
                    v.wait_ge(p.sem("mm"), p.cnt["mm"])
                    v.wait_ge(p.sem("ld_res"), p.cnt["ld_res"])
                    if gg >= 2:
                        v.wait_ge(p.sem("st_out"), 16 * (gg - 1))
                    p.inc(v.scalar_tensor_tensor(ob[:, (gg % 2) * 512:(gg % 2 + 1) * 512],
                                                 ps[gg % 4][:], bov[:, md:md + 1],
                                                 xsb[:, (gg % 2) * 512:(gg % 2 + 1) * 512],
                                                 ALU.add, ALU.add), "v_o", 1)
                    p.inc(v.engine_nop(), f"evb{gg % 4}", 1)

                @block.gpsimd
                def _(g, b=b, md=md, nb4=nb4, c0=c0):
                    gg = p.cnt["p6g"] - 1
                    g.wait_ge(p.sem("v_o"), p.cnt["v_o"])
                    p.inc(g.dma_start(out=xout[md * 128:(md + 1) * 128,
                                               c0 + nb4 * 512:c0 + (nb4 + 1) * 512],
                                      in_=ob[:, (gg % 2) * 512:(gg % 2 + 1) * 512]), "st_out", 16)

    @block.sync
    def _(s):
        s.wait_ge(p.sem("st_out"), p.cnt["st_out"])

    p.close()
    return p.nc


def run_gru(x_full, inp):
    nc = build_gru_program()
    shards = to_fm_B(x_full)
    base = {
        "W_hg": np.ascontiguousarray(inp["L1_attn_W_hg"]),
        "W_out1": np.ascontiguousarray(inp["L1_attn_W_out"]),
        "b_hg": vec_chunks(inp["L1_attn_b_hg"]),
        "b_out": vec_chunks(inp["L1_attn_b_out"]),
        "h0": vec_chunks(inp["L1_attn_h0"]),
    }
    in_maps = [dict(base, xin=s) for s in shards]
    res = run_bass_kernel_spmd(nc, in_maps, core_ids=list(range(NCORES)))
    return from_fm_B([res.results[j]["xout"] for j in range(NCORES)])



def to_fm_P(x):
    out = []
    for j in range(NCORES):
        xs = x[:, j * PPC:(j + 1) * PPC, :]
        out.append(np.ascontiguousarray(xs.reshape(TOK, D).T))
    return out


def from_fm_P(shards):
    x = np.empty((BATCH, SEQ, D), np.float32)
    for j, s in enumerate(shards):
        x[:, j * PPC:(j + 1) * PPC, :] = s.T.reshape(BATCH, PPC, D)
    return x


def run_ffn(x_full, inp, debug=False):
    nc = build_ffn_program(debug)
    shards = to_fm_P(x_full)
    base = {w: np.ascontiguousarray(inp[f"L0_ffn_{w}"]) for w in
            ("wf", "wi", "wc", "uf", "ui", "uc", "wo", "uo")}
    for v in ("bf", "bi", "bc", "bo", "in_c"):
        base[v] = vec_chunks(inp[f"L0_ffn_{v}"])
    in_maps = [dict(base, xin=s) for s in shards]
    res = run_bass_kernel_spmd(nc, in_maps, core_ids=list(range(NCORES)))
    out = from_fm_P([res.results[j]["xout"] for j in range(NCORES)])
    if debug:
        return out, res.results
    return out


def kernel(**inputs):
    x = np.asarray(inputs["L0_attn_hidden_states"], np.float32)
    x = run_ffn(x, inputs)
    x = run_mamba(x, inputs)
    x = run_ffn(x, inputs)
    x = run_moe(x, inputs, pshard=True)
    x = run_gru(x, inputs)
    x = run_moe(x, inputs, pshard=False)
    return x
